# version 10 (final)
# baseline (speedup 1.0000x reference)
import numpy as np
import ml_dtypes
import concourse.bass as bass
import concourse.mybir as mybir
from concourse.bass_utils import run_bass_kernel_spmd
from contextlib import ExitStack

F32 = mybir.dt.float32
BF16 = mybir.dt.bfloat16
I32 = mybir.dt.int32
AF = mybir.ActivationFunctionType
ALU = mybir.AluOpType
AX = mybir.AxisListType

NCORES = 8
BSEQ, T = 4, 2048
NT = BSEQ * T
DM, DH = 1024, 4096
H0, D0 = 16, 64
H1, D1 = 8, 128
CS, NCH = 128, 16
KC, CC = DM // 128, DH // 128
BF_NP = ml_dtypes.bfloat16
FUSED = True
P6_MODE = "full"
CONST_VALS = [1e-5, 1e-24, -0.5, 0.5, 2.0, 127.0, -1e30, -4.505456673639644]

ALL_PHASES = ["prep", "p1", "p2", "p3", "p3b", "p4", "p4c", "p5", "p5b", "p6", "p7a", "p7b"]

TENSORS = {
    "x":        ([NT, DM], F32, None),
    "wq0":      ([DM, DM], F32, None), "wk0": ([DM, DM], F32, None),
    "wv0":      ([DM, DM], F32, None), "wo0": ([DM, DM], F32, None),
    "alpha":    ([1, H0], F32, None), "beta": ([1, H0], F32, None),
    "w1T":      ([DM, DH], F32, None), "w2T": ([DH, DM], F32, None),
    "convw":    ([128, CC * 3], F32, None), "convb": ([128, CC], F32, None),
    "wq1":      ([DM, DM], F32, None), "wk1": ([DM, DM], F32, None),
    "wv1":      ([DM, DM], F32, None), "wo1": ([DM, DM], F32, None),
    "bo1":      ([1, DM], F32, None), "theta": ([1, H1], F32, None),
    "gamma":    ([1, DM], F32, None), "betaln": ([1, DM], F32, None),
    "basewT":   ([DM, DM], F32, None), "baseb": ([1, DM], F32, None),
    "routlora": ([DM, 16], F32, None), "bcat": ([64, DM], F32, None),
    "routb": ([1, 8], F32, None),
    "c_ident":  ([128, 128], BF16, None),
    "c_SL":     ([128, 128], F32, None), "c_SU": ([128, 128], F32, None),
    "c_D0":     ([128, 128], F32, None), "c_causal01": ([128, 128], F32, None),
    "c_negdiag": ([128, 128], F32, None),
    "c_onescol": ([128, 1], F32, None), "c_onesrow": ([1, 128], F32, None),
    "c_consts": ([128, 8], F32, None),
    "wq0b": ([DM, DM], BF16, "prep"), "wk0b": ([DM, DM], BF16, "prep"),
    "wv0b": ([DM, DM], BF16, "prep"), "wo0b": ([DM, DM], BF16, "prep"),
    "wq1b": ([DM, DM], BF16, "prep"), "wk1b": ([DM, DM], BF16, "prep"),
    "wv1b": ([DM, DM], BF16, "prep"), "wo1b": ([DM, DM], BF16, "prep"),
    "basewb": ([DM, DM], BF16, "prep"), "routlorab": ([DM, 16], BF16, "prep"),
    "bcatb": ([64, DM], BF16, "prep"),
    "w1qb": ([DM, DH], BF16, "prep"), "w2qb": ([DH, DM], BF16, "prep"),
    "miscf": ([1, 64], F32, "prep"),
    "xn0b": ([NT, DM], BF16, "p1"),
    "xq0b": ([NT, DM], BF16, "p1"),
    "isxT": ([NT // 128, 128], F32, "p1"),
    "qh0f": ([BSEQ, T, DM], BF16, "p2"),
    "kh0f": ([BSEQ, T, DM], BF16, "p2"),
    "kh0": ([BSEQ, H0, T, D0], BF16, "p2"),
    "vh0": ([BSEQ, H0, T, D0], BF16, "p2"),
    "ytm0": ([NT, DM], BF16, "p3"),
    "attn0": ([NT, DM], F32, "p3b"),
    "hbf": ([CC, 128, NT], BF16, "p4"),
    "x1": ([NT, DM], F32, "p4c"),
    "mxrow": ([1, NT], F32, "p4"),
    "ishc_h": ([128, 64], F32, "p4"),
    "xn1b": ([NT, DM], BF16, "p5"),
    "q1f": ([BSEQ, DM, T], BF16, "p5b"),
    "k1f": ([BSEQ, DM, T], BF16, "p5b"),
    "v1h": ([BSEQ, H1, T, D1], BF16, "p5b"),
    "obh": ([NT, DM], BF16, "p6"),
    "oln1b": ([NT, DM], BF16, "p6"),
    "attn1": ([NT, DM], F32, "p7a"),
    "out": ([NT, DM], F32, "p7b"),
}


def host_constants():
    i = np.arange(128, dtype=np.float32)
    return {
        "c_ident": np.eye(128).astype(BF_NP),
        "c_SL": np.tril(np.ones((128, 128), np.float32), -1),
        "c_SU": np.triu(np.ones((128, 128), np.float32), 1),
        "c_D0": (i[:, None] - i[None, :]).astype(np.float32),
        "c_causal01": (i[:, None] >= i[None, :]).astype(np.float32),
        "c_negdiag": np.where(i[:, None] >= i[None, :], 0.0, -1e30).astype(np.float32),
        "c_onescol": np.ones((128, 1), np.float32),
        "c_onesrow": np.ones((1, 128), np.float32),
        "c_consts": np.broadcast_to(np.array(
            CONST_VALS, np.float32), (128, len(CONST_VALS))).copy(),
    }


class Ctx:
    def __init__(self, nc, block, dram, es):
        self.nc, self.block, self.dram, self.es = nc, block, dram, es
        self.sems, self.cnt = {}, {}
        self.pb_total = 0
        self.uid = 0

    def sem(self, name):
        if name not in self.sems:
            self.sems[name] = self.es.enter_context(self.nc.semaphore(name))
            self.cnt[name] = 0
        return self.sems[name]

    def fresh(self, tag):
        self.uid += 1
        return f"{tag}{self.uid}"


def bcast_col(ap_col, width):
    return bass.AP(ap_col.tensor, ap_col.offset,
                   [list(q) for q in ap_col.ap[:1]] + [[0, width]])


def run_pipe(ctx, n, stages, bufs=2):
    block = ctx.block
    engs = [s[0] for s in stages]
    assert len(set(engs)) == len(engs), f"duplicate engines in pipe: {engs}"
    names = [(f"dma_{s[0]}" if s[2] == "dma" else f"stgc{k}") for k, s in enumerate(stages)]
    incs = [(16 * s[3]) if s[2] == "dma" else 1 for s in stages]
    bases = [ctx.cnt.get(nm, 0) if nm in ctx.sems else (ctx.sem(nm), ctx.cnt[nm])[1]
             for nm in names]
    bases = []
    for nm in names:
        ctx.sem(nm)
        bases.append(ctx.cnt[nm])
    pb = ctx.sem("pb")
    pbt = ctx.pb_total

    def make_body(k):
        eng_name, fn, kind, m = stages[k]
        sem_self = ctx.sem(names[k])
        sem_prev = ctx.sem(names[k - 1]) if k > 0 else None
        sem_last = ctx.sem(names[-1])

        def body(eng):
            eng.wait_ge(pb, pbt)
            for i in range(n):
                if k == 0 and i >= bufs:
                    eng.wait_ge(sem_last, bases[-1] + incs[-1] * (i - bufs + 1))
                if k > 0:
                    eng.wait_ge(sem_prev, bases[k - 1] + incs[k - 1] * (i + 1))
                r = fn(eng, i)
                rl = r if isinstance(r, (list, tuple)) else [r]
                rl = [x for x in rl if x is not None]
                if kind == "dma":
                    assert len(rl) == m, f"dma stage expected {m} instrs got {len(rl)}"
                    for x in rl:
                        x.then_inc(sem_self, 16)
                else:
                    rl[-1].then_inc(sem_self, 1)
            if k == len(stages) - 1:
                eng.wait_ge(sem_self, bases[k] + incs[k] * n)
                eng.sem_inc(pb, 1)
        return body

    for k in range(len(stages)):
        getattr(block, stages[k][0])(make_body(k))
        ctx.cnt[names[k]] += incs[k] * n
    ctx.pb_total += 1


def solo(ctx, eng_name, fn):
    pb = ctx.sem("pb")
    pbt = ctx.pb_total
    nm = "solo_s"
    ctx.sem(nm)
    base = ctx.cnt[nm]
    ctx.cnt[nm] += 1

    def body(eng):
        eng.wait_ge(pb, pbt)
        last = fn(eng)
        if last is not None:
            last.then_inc(ctx.sem(nm), 1)
            eng.wait_ge(ctx.sem(nm), base + 1)
        else:
            ctx.cnt[nm] -= 0
            eng.sem_inc(ctx.sem(nm), 1)
        eng.sem_inc(pb, 1)
    getattr(ctx.block, eng_name)(body)
    ctx.pb_total += 1


def sbuf(ctx, name, shape, dtype):
    return ctx.phase_stack.enter_context(ctx.nc.sbuf_tensor(name, shape, dtype))[:]


def psum_t(ctx, name, shape, dtype=F32):
    return ctx.phase_stack.enter_context(ctx.nc.psum_tensor(name, shape, dtype))[:]


def emit_prep(ctx):
    nc, dram = ctx.nc, ctx.dram
    slab_f = sbuf(ctx, "pr_f", [128, 2 * DH], F32)
    slab_b = sbuf(ctx, "pr_b", [128, 2 * DH], BF16)
    slab_i = sbuf(ctx, "pr_i", [128, 2 * DH], I32)
    asum = sbuf(ctx, "pr_asum", [128, 1], F32)
    acc1 = sbuf(ctx, "pr_acc1", [128, 1], F32)
    acc2 = sbuf(ctx, "pr_acc2", [128, 1], F32)
    onescol = sbuf(ctx, "pr_ones", [128, 1], F32)
    onesrow = sbuf(ctx, "pr_onesr", [1, 128], F32)
    swcol = sbuf(ctx, "pr_sw", [128, 2], F32)
    misc = sbuf(ctx, "pr_misc", [1, 64], F32)
    ab_s = sbuf(ctx, "pr_ab", [1, 32], F32)
    ps_r = psum_t(ctx, "pr_ps", [128, 128])

    def fslot(i):
        return slab_f[:, (i % 2) * DH:(i % 2) * DH + DH]

    def bslot(i):
        return slab_b[:, (i % 2) * DH:(i % 2) * DH + DH]

    def ld_small(eng):
        a = eng.dma_start(out=onescol, in_=dram["c_onescol"])
        b = eng.dma_start(out=onesrow, in_=dram["c_onesrow"])
        c = eng.dma_start(out=ab_s[:, 0:16], in_=dram["alpha"])
        d = eng.dma_start(out=ab_s[:, 16:32], in_=dram["beta"])
        return [a, b, c, d]
    run_pipe(ctx, 1, [("sync", lambda e, i: ld_small(e), "dma", 4)])

    def sig(eng):
        eng.memzero(misc)
        eng.activation(misc[0:1, 0:32], ab_s[0:1, 0:32], AF.Sigmoid)
        return eng.mul(misc[0:1, 0:32], misc[0:1, 0:32], 0.5)
    solo(ctx, "scalar", sig)

    casts = [("wq0", "wq0b", 128, DM, 8), ("wk0", "wk0b", 128, DM, 8),
             ("wv0", "wv0b", 128, DM, 8), ("wo0", "wo0b", 128, DM, 8),
             ("wq1", "wq1b", 128, DM, 8), ("wk1", "wk1b", 128, DM, 8),
             ("wv1", "wv1b", 128, DM, 8), ("wo1", "wo1b", 128, DM, 8),
             ("basewT", "basewb", 128, DM, 8), ("routlora", "routlorab", 128, 16, 8),
             ("bcat", "bcatb", 64, DM, 1)]
    plan = []
    for s, d, pr, c, nsl in casts:
        for k in range(nsl):
            plan.append((s, d, pr, c, k))

    def c_load(eng, i):
        s, d, pr, c, k = plan[i]
        return eng.dma_start(out=fslot(i)[0:pr, 0:c],
                             in_=dram[s].rearrange("(n p) c -> n p c", p=pr)[k])

    def c_cast(eng, i):
        s, d, pr, c, k = plan[i]
        return eng.tensor_copy(bslot(i)[0:pr, 0:c], fslot(i)[0:pr, 0:c])

    def c_store(eng, i):
        s, d, pr, c, k = plan[i]
        return eng.dma_start(out=dram[d].rearrange("(n p) c -> n p c", p=pr)[k],
                             in_=bslot(i)[0:pr, 0:c])
    run_pipe(ctx, len(plan), [("sync", c_load, "dma", 1),
                              ("vector", c_cast, "c", 1),
                              ("gpsimd", c_store, "dma", 1)])

    for j, (srcn, dstn, nsl, cols, slot) in enumerate(
            [("w1T", "w1qb", 8, DH, 32), ("w2T", "w2qb", 32, DM, 33)]):
        def s_load(eng, i, srcn=srcn, cols=cols):
            return eng.dma_start(out=fslot(i)[:, 0:cols],
                                 in_=dram[srcn].rearrange("(n p) c -> n p c", p=128)[i])

        def s_acc(eng, i, nsl=nsl, cols=cols):
            a = acc1 if i % 2 == 0 else acc2
            eng.tensor_reduce(a, fslot(i)[:, 0:cols], AX.X, ALU.add,
                              apply_absolute_value=True)
            SPACER["v"](eng)
            if i == 0:
                return eng.tensor_copy(asum, a)
            return eng.tensor_add(asum, asum, a)
        run_pipe(ctx, nsl, [("sync", s_load, "dma", 1), ("vector", s_acc, "c", 1)])

        def x_sum(eng):
            return eng.matmul(ps_r[0:1, 0:1], onescol, asum, start=True, stop=True)
        solo(ctx, "tensor", x_sum)

        def x_mean(eng, j=j, cols=cols, nsl=nsl, slot=slot):
            numel = float(128 * nsl * cols)
            eng.tensor_scalar(acc1[0:1, 0:1], ps_r[0:1, 0:1], 1.0 / numel, 1e-5,
                              ALU.mult, ALU.max)
            SPACER["v"](eng)
            eng.tensor_copy(misc[0:1, slot:slot + 1], acc1[0:1, 0:1])
            SPACER["v"](eng)
            return eng.reciprocal(acc1[0:1, 0:1], acc1[0:1, 0:1])
        solo(ctx, "vector", x_mean)

        def x_bcast(eng):
            return eng.matmul(ps_r[:, 0:1], onesrow, acc1[0:1, 0:1], start=True, stop=True)
        solo(ctx, "tensor", x_bcast)

        def x_evac(eng, j=j):
            return eng.tensor_copy(swcol[:, j:j + 1], ps_r[:, 0:1])
        solo(ctx, "vector", x_evac)

        def q_load(eng, i, srcn=srcn, cols=cols):
            return eng.dma_start(out=fslot(i)[:, 0:cols],
                                 in_=dram[srcn].rearrange("(n p) c -> n p c", p=128)[i])

        def q_quant(eng, i, j=j, cols=cols):
            sl = fslot(i)[:, 0:cols]
            isl = slab_i[:, (i % 2) * DH:(i % 2) * DH + cols]
            eng.tensor_scalar(sl, sl, swcol[:, j:j + 1], 1.0, ALU.mult, ALU.min)
            return eng.tensor_scalar(isl, sl, -1.0, None, ALU.max)

        def q_cast(eng, i, cols=cols):
            return eng.copy(bslot(i)[:, 0:cols], slab_i[:, (i % 2) * DH:(i % 2) * DH + cols])

        def q_store(eng, i, dstn=dstn, cols=cols):
            return eng.dma_start(out=dram[dstn].rearrange("(n p) c -> n p c", p=128)[i],
                                 in_=bslot(i)[:, 0:cols])
        run_pipe(ctx, nsl, [("sync", q_load, "dma", 1),
                            ("vector", q_quant, "c", 1),
                            ("scalar", q_cast, "c", 1),
                            ("gpsimd", q_store, "dma", 1)])

    def misc_store(eng, i):
        return eng.dma_start(out=dram["miscf"], in_=misc)
    run_pipe(ctx, 1, [("gpsimd", misc_store, "dma", 1)])


def emit_ln(ctx, src_name, xn_name, quant):
    nc, dram = ctx.nc, ctx.dram
    tag = "l0" if quant else "l1"
    xt = sbuf(ctx, f"{tag}_x", [128, 2 * DM], F32)
    sq = sbuf(ctx, f"{tag}_sq", [128, DM], F32)
    xn = sbuf(ctx, f"{tag}_xn", [128, 2 * DM], F32)
    xq = sbuf(ctx, f"{tag}_xq", [128, 2 * DM], F32) if quant else None
    xqi = sbuf(ctx, f"{tag}_xqi", [128, DM], I32) if quant else None
    stats = sbuf(ctx, f"{tag}_st", [128, 2, 8], F32)
    isxcols = sbuf(ctx, f"{tag}_isx", [128, 64], F32) if quant else None
    ident = sbuf(ctx, f"{tag}_id", [128, 128], BF16) if quant else None
    ps_i = psum_t(ctx, f"{tag}_psi", [64, 128]) if quant else None
    NTT = NT // 128

    if quant:
        run_pipe(ctx, 1, [("sync", lambda e, i: e.dma_start(out=ident, in_=dram["c_ident"]),
                           "dma", 1)])

    def slot(buf, i):
        return buf[:, (i % 2) * DM:(i % 2) * DM + DM]

    def l_load(eng, i):
        return eng.dma_start(out=slot(xt, i),
                             in_=dram[src_name].rearrange("(n p) c -> n p c", p=128)[i])

    def l_stats(eng, i):
        x = slot(xt, i)
        eng.activation(sq, x, AF.Copy, accum_out=stats[:, i % 2, 0:1])
        eng.activation(sq, x, AF.Square, accum_out=stats[:, i % 2, 1:2])
        eng.mul(stats[:, i % 2, 2:3], stats[:, i % 2, 0:1], -1.0 / DM)
        eng.mul(stats[:, i % 2, 6:7], stats[:, i % 2, 1:2], 1.0 / DM)
        eng.mul(stats[:, i % 2, 3:4], stats[:, i % 2, 2:3], stats[:, i % 2, 2:3])
        SPACER["s"](eng)
        eng.mul(stats[:, i % 2, 5:6], stats[:, i % 2, 3:4], -1.0)
        SPACER["s"](eng)
        eng.add(stats[:, i % 2, 7:8], stats[:, i % 2, 6:7], stats[:, i % 2, 5:6])
        SPACER["s"](eng)
        eng.activation(stats[:, i % 2, 3:4], stats[:, i % 2, 7:8], AF.Ln, bias=1e-5)
        SPACER["s"](eng)
        return eng.activation(stats[:, i % 2, 4:5], stats[:, i % 2, 3:4], AF.Exp, scale=-0.5)

    def l_xn(eng, i):
        x = slot(xt, i)
        o = slot(xn, i)
        r = eng.tensor_scalar(o, x, stats[:, i % 2, 2:3], stats[:, i % 2, 4:5], ALU.add, ALU.mult)
        if not quant:
            return r
        eng.tensor_reduce(stats[:, i % 2, 5:6], o, AX.X, ALU.max, apply_absolute_value=True)
        SPACER["v"](eng)
        eng.tensor_scalar(stats[:, i % 2, 0:1], stats[:, i % 2, 5:6], 1e-5, None, ALU.max)
        SPACER["v"](eng)
        eng.tensor_scalar(isxcols[:, i:i + 1], stats[:, i % 2, 0:1], 1.0 / 127.0, None, ALU.mult)
        eng.reciprocal(stats[:, i % 2, 1:2], stats[:, i % 2, 0:1])
        SPACER["v"](eng)
        eng.tensor_scalar(stats[:, i % 2, 2:3], stats[:, i % 2, 1:2], 127.0, None, ALU.mult)
        SPACER["v"](eng)
        eng.tensor_tensor(slot(xq, i), o, bcast_col(stats[:, i % 2, 2:3], DM), ALU.mult)
        eng.tensor_scalar(slot(xq, i), slot(xq, i), 127.0, None, ALU.min)
        eng.tensor_scalar(xqi, slot(xq, i), -128.0, None, ALU.max)
        SPACER["v"](eng)
        return eng.tensor_copy(slot(xq, i), xqi)

    def l_store(eng, i):
        a = eng.dma_start(out=dram[xn_name].rearrange("(n p) c -> n p c", p=128)[i],
                          in_=slot(xn, i))
        if not quant:
            return [a]
        b = eng.dma_start(out=dram["xq0b"].rearrange("(n p) c -> n p c", p=128)[i],
                          in_=slot(xq, i))
        return [a, b]

    run_pipe(ctx, NTT, [("sync", l_load, "dma", 1),
                        ("scalar", l_stats, "c", 1),
                        ("vector", l_xn, "c", 1),
                        ("gpsimd", l_store, "dma", 2 if quant else 1)])

    if quant:
        identf = sbuf(ctx, f"{tag}_idf", [128, 128], F32)

        def mk_idf(eng):
            return eng.copy(identf, ident)
        solo(ctx, "scalar", mk_idf)

        def t_tr2(eng):
            return eng.transpose(ps_i[0:64, 0:128], isxcols[:, 0:64], identf)
        solo(ctx, "tensor", t_tr2)

        def isx_evac(eng):
            return eng.tensor_copy(xt[0:64, 0:128], ps_i[0:64, 0:128])
        solo(ctx, "vector", isx_evac)

        def isx_store(eng):
            nm = ctx.fresh("ix")
            d = eng.dma_start(out=dram["isxT"], in_=xt[0:64, 0:128])
            d.then_inc(ctx.sem(nm), 16)
            eng.wait_ge(ctx.sem(nm), 16)
            return None
        solo(ctx, "gpsimd", isx_store)


def emit_qkv0(ctx):
    nc, dram = ctx.nc, ctx.dram
    wq = sbuf(ctx, "p2_wq", [128, KC, DM], BF16)
    wk = sbuf(ctx, "p2_wk", [128, KC, DM], BF16)
    wv = sbuf(ctx, "p2_wv", [128, KC, DM], BF16)
    xnf = sbuf(ctx, "p2_xnf", [128, KC, T], BF16)
    kbuf = sbuf(ctx, "p2_kbuf", [128, 16, DM], F32)
    qv_b = sbuf(ctx, "p2_qvb", [128, 2 * DM], BF16)
    ss = sbuf(ctx, "p2_ss", [128, 16 * H0], F32)
    rn = sbuf(ctx, "p2_rn", [128, 16 * H0], F32)
    rn2 = sbuf(ctx, "p2_rn2", [128, 16 * H0], F32)
    knb = sbuf(ctx, "p2_knb", [128, 2 * DM], BF16)
    ksq = sbuf(ctx, "p2_ksq", [128, DM], F32)
    ps = [psum_t(ctx, f"p2_ps{i}", [128, 512]) for i in range(4)]

    wload = [("wq0b", wq), ("wk0b", wk), ("wv0b", wv)]
    run_pipe(ctx, 3, [("sync", lambda e, i: e.dma_start(
        out=wload[i][1], in_=dram[wload[i][0]].rearrange("(kc p) o -> p kc o", p=128)),
        "dma", 1)], bufs=3)

    for seq in range(BSEQ):
        def tr_load(eng, i, seq=seq):
            return eng.dma_start_transpose(
                xnf[:, i, :],
                dram["xn0b"][seq * T:(seq + 1) * T, i * 128:(i + 1) * 128])
        run_pipe(ctx, KC, [("scalar", tr_load, "dma", 1)], bufs=KC)

        items = [(p, t) for t in range(16) for p in (0, 2)]

        def mm_fn(eng, i, seq=seq):
            p, t = items[i]
            w = [wq, wk, wv][p]
            r = None
            for oh in range(2):
                bank = ps[(i % 2) * 2 + oh]
                for kc in range(KC):
                    r = eng.matmul(bank, xnf[:, kc, t * 128:(t + 1) * 128],
                                   w[:, kc, oh * 512:(oh + 1) * 512],
                                   start=(kc == 0), stop=(kc == KC - 1))
            return r

        def ev_fn(eng, i, seq=seq):
            p, t = items[i]
            sl = qv_b[:, (i % 2) * DM:(i % 2) * DM + DM]
            eng.tensor_copy(sl[:, 0:512], ps[(i % 2) * 2])
            return eng.tensor_copy(sl[:, 512:1024], ps[(i % 2) * 2 + 1])

        def st_fn(eng, i, seq=seq):
            p, t = items[i]
            sl = qv_b[:, (i % 2) * DM:(i % 2) * DM + DM]
            if p == 0:
                return eng.dma_start(
                    out=dram["qh0f"][seq][t * 128:(t + 1) * 128, :], in_=sl)
            return eng.dma_start(
                out=dram["vh0"][seq][:, t * 128:(t + 1) * 128, :].rearrange("h p d -> p h d"),
                in_=sl.rearrange("p (h d) -> p h d", h=H0))
        run_pipe(ctx, len(items), [("tensor", mm_fn, "c", 1),
                                   ("vector", ev_fn, "c", 1),
                                   ("gpsimd", st_fn, "dma", 1)])

        def kmm_fn(eng, t, seq=seq):
            r = None
            for oh in range(2):
                bank = ps[(t % 2) * 2 + oh]
                for kc in range(KC):
                    r = eng.matmul(bank, xnf[:, kc, t * 128:(t + 1) * 128],
                                   wk[:, kc, oh * 512:(oh + 1) * 512],
                                   start=(kc == 0), stop=(kc == KC - 1))
            return r

        def kev_fn(eng, t, seq=seq):
            eng.tensor_copy(kbuf[:, t, 0:512], ps[(t % 2) * 2])
            return eng.tensor_copy(kbuf[:, t, 512:1024], ps[(t % 2) * 2 + 1])
        run_pipe(ctx, 16, [("tensor", kmm_fn, "c", 1),
                           ("vector", kev_fn, "c", 1)])

        def kn_sq(eng, i, seq=seq):
            eng.tensor_mul(ksq, kbuf[:, i, :], kbuf[:, i, :])
            return eng.tensor_reduce(
                ss.rearrange("p (t h) -> p t h", h=H0)[:, i:i + 1, :].rearrange("p o h -> p (o h)").rearrange("p (h o) -> p h o", o=1),
                ksq.rearrange("p (h d) -> p h d", h=H0), AX.X, ALU.add)

        def kn_rn(eng, i, seq=seq):
            s = ss[:, i * H0:(i + 1) * H0]
            r = rn[:, i * H0:(i + 1) * H0]
            r2 = rn2[:, i * H0:(i + 1) * H0]
            eng.activation(r2, s, AF.Ln, bias=1e-24)
            SPACER["s"](eng)
            return eng.activation(r, r2, AF.Exp, scale=-0.5)

        def kn_mul(eng, i, seq=seq):
            sl = knb[:, (i % 2) * DM:(i % 2) * DM + DM]
            rv = rn[:, i * H0:(i + 1) * H0]
            r3 = bass.AP(rv.tensor, rv.offset,
                         [list(q) for q in rv.ap[:1]] + [[1, H0], [0, D0]])
            return eng.tensor_tensor(sl.rearrange("p (h d) -> p h d", h=H0),
                                     kbuf[:, i, :].rearrange("p (h d) -> p h d", h=H0),
                                     r3, ALU.mult)

        def kn_st(eng, i, seq=seq):
            sl = knb[:, (i % 2) * DM:(i % 2) * DM + DM]
            a = eng.dma_start(
                out=dram["kh0"][seq][:, i * 128:(i + 1) * 128, :].rearrange("h p d -> p h d"),
                in_=sl.rearrange("p (h d) -> p h d", h=H0))
            b = eng.dma_start(out=dram["kh0f"][seq][i * 128:(i + 1) * 128, :], in_=sl)
            return [a, b]
        run_pipe(ctx, 16, [("vector", kn_sq, "c", 1),
                           ("scalar", kn_rn, "c", 1),
                           ("gpsimd", kn_mul, "c", 1),
                           ("sync", kn_st, "dma", 2)])


PHASE_EMITTERS = {}


SPACER = {}


def emit_consts(ctx):
    cst = ctx.persist.enter_context(ctx.nc.sbuf_tensor("g_consts", [128, 8], F32))[:]
    gsp = ctx.persist.enter_context(ctx.nc.sbuf_tensor("g_sp", [128, 24], F32))[:]
    SPACER["v"] = lambda eng: eng.tensor_copy(gsp[:, 0:8], cst)
    SPACER["s"] = lambda eng: eng.copy(gsp[:, 8:16], cst)
    SPACER["g"] = lambda eng: eng.tensor_copy(gsp[:, 16:24], cst)

    def ld(eng, i):
        return eng.dma_start(out=cst, in_=ctx.dram["c_consts"])
    run_pipe(ctx, 1, [("sync", ld, "dma", 1)])
    for j, v in enumerate(CONST_VALS):
        ctx.nc.const_aps.register(F32, float(v), cst[:, j:j + 1])


def build_program(phases):
    nc = bass.Bass("TRN2", target_bir_lowering=False, debug=False,
                   detect_race_conditions=False)
    sel = set(phases)
    dram = {}
    for name, (shape, dt, prod) in TENSORS.items():
        if prod is None:
            kind = "ExternalInput"
        elif prod in sel:
            kind = "Internal" if (FUSED and name != "out") else "ExternalOutput"
            if name == "out":
                kind = "ExternalOutput"
        else:
            kind = "ExternalInput"
        if not FUSED and prod in sel:
            kind = "ExternalOutput"
        dram[name] = nc.dram_tensor(name, shape, dt, kind=kind).ap()
    es = ExitStack()
    with nc.Block() as block:
        ctx = Ctx(nc, block, dram, es)
        ctx.persist = es
        ctx.phase_stack = es
        emit_consts(ctx)
        for ph in phases:
            st = ExitStack()
            ctx.phase_stack = st
            PHASE_EMITTERS[ph](ctx)
            st.close()
        @block.sync
        def _(sync):
            sync.wait_ge(ctx.sem("pb"), ctx.pb_total)
    es.close()
    return nc


def host_inputs(inp, core):
    x = np.asarray(inp["L0_attn_x"], np.float32)
    m = {
        "x": np.ascontiguousarray(x[core * BSEQ:(core + 1) * BSEQ].reshape(NT, DM)),
        "wq0": inp["L0_attn_Wq"], "wk0": inp["L0_attn_Wk"],
        "wv0": inp["L0_attn_Wv"], "wo0": inp["L0_attn_Wo"],
        "alpha": np.asarray(inp["L0_attn_alpha_raw"], np.float32).reshape(1, H0),
        "beta": np.asarray(inp["L0_attn_beta_raw"], np.float32).reshape(1, H0),
        "w1T": np.ascontiguousarray(np.asarray(inp["L0_ffn_w1"], np.float32).T),
        "w2T": np.ascontiguousarray(np.asarray(inp["L0_ffn_w2"], np.float32).T),
        "convw": np.ascontiguousarray(
            np.asarray(inp["L0_ffn_conv_w"], np.float32).reshape(CC, 128, 3)
            .transpose(1, 0, 2).reshape(128, CC * 3)),
        "convb": np.ascontiguousarray(
            np.asarray(inp["L0_ffn_conv_b"], np.float32).reshape(CC, 128).T),
        "wq1": inp["L1_attn_Wq"], "wk1": inp["L1_attn_Wk"],
        "wv1": inp["L1_attn_Wv"], "wo1": inp["L1_attn_Wo"],
        "bo1": np.asarray(inp["L1_attn_bo"], np.float32).reshape(1, DM),
        "theta": np.asarray(inp["L1_attn_theta"], np.float32).reshape(1, H1),
        "gamma": np.asarray(inp["L1_attn_gamma"], np.float32).reshape(1, DM),
        "betaln": np.asarray(inp["L1_attn_beta"], np.float32).reshape(1, DM),
        "basewT": np.ascontiguousarray(np.asarray(inp["L1_ffn_base_w"], np.float32).T),
        "baseb": np.asarray(inp["L1_ffn_base_b"], np.float32).reshape(1, DM),
        "routlora": np.ascontiguousarray(np.concatenate(
            [np.asarray(inp["L1_ffn_router_w"], np.float32).T,
             np.asarray(inp["L1_ffn_lora_A"], np.float32)], axis=1)),
        "bcat": np.ascontiguousarray(
            np.asarray(inp["L1_ffn_lora_B"], np.float32).reshape(64, DM)),
        "routb": np.asarray(inp["L1_ffn_router_b"], np.float32).reshape(1, 8),
    }
    m.update(host_constants())
    for k in list(m):
        arr = np.asarray(m[k])
        want = TENSORS[k][1]
        if want == F32 and arr.dtype != np.float32:
            m[k] = arr.astype(np.float32)
    return m


def kernel(**inputs):
    nc = build_program(ALL_PHASES)
    in_maps = [host_inputs(inputs, c) for c in range(NCORES)]
    res = run_bass_kernel_spmd(nc, in_maps, core_ids=list(range(NCORES)))
    outs = [res.results[c]["out"].reshape(BSEQ, T, DM) for c in range(NCORES)]
    return np.concatenate(outs, axis=0)


PHASE_EMITTERS.update({
    "prep": emit_prep,
    "p1": lambda ctx: emit_ln(ctx, "x", "xn0b", True),
    "p2": emit_qkv0,
    "p5": lambda ctx: emit_ln(ctx, "x1", "xn1b", False),
})


def emit_scan(ctx):
    nc, dram, block = ctx.nc, ctx.dram, ctx.block
    SLm = sbuf(ctx, "p3_SL", [128, 128], F32)
    SUm = sbuf(ctx, "p3_SU", [128, 128], F32)
    misc = sbuf(ctx, "p3_misc", [1, 64], F32)
    onesrow = sbuf(ctx, "p3_or", [1, 128], F32)
    abcols = sbuf(ctx, "p3_ab", [128, 48], F32)
    nSL = sbuf(ctx, "p3_nSL", [128, H0 * 128], F32)
    nSU = sbuf(ctx, "p3_nSU", [128, H0 * 128], F32)
    bank = [psum_t(ctx, f"p3_bk{i}", [128, 128]) for i in range(8)]
    ps_b = bank[0]

    def pl_ld(eng, i):
        return [eng.dma_start(out=SLm, in_=dram["c_SL"]),
                eng.dma_start(out=SUm, in_=dram["c_SU"]),
                eng.dma_start(out=misc, in_=dram["miscf"]),
                eng.dma_start(out=onesrow, in_=dram["c_onesrow"])]
    run_pipe(ctx, 1, [("sync", pl_ld, "dma", 4)])

    def pl_bc(eng):
        return eng.matmul(ps_b[:, 0:32], onesrow, misc[0:1, 0:32], start=True, stop=True)
    solo(ctx, "tensor", pl_bc)

    def pl_ev(eng):
        eng.tensor_scalar(abcols[:, 0:16], ps_b[:, 0:16], -1.0, None, ALU.mult)
        eng.tensor_copy(abcols[:, 16:32], ps_b[:, 16:32])
        eng.tensor_copy(abcols[:, 32:48], ps_b[:, 0:16])
        for h in range(H0):
            eng.tensor_scalar(nSL[:, h * 128:(h + 1) * 128], SLm,
                              abcols[:, h:h + 1], None, ALU.mult)
            r = eng.tensor_scalar(nSU[:, h * 128:(h + 1) * 128], SUm,
                                  abcols[:, h:h + 1], None, ALU.mult)
        return r
    solo(ctx, "vector", pl_ev)

    kfm = sbuf(ctx, "p3_kfm", [128, 2, T], BF16)
    qfm = sbuf(ctx, "p3_qfm", [128, 2, T], BF16)
    kfmB = sbuf(ctx, "p3_kfmB", [64, 2, T], BF16)
    qfmB = sbuf(ctx, "p3_qfmB", [64, 2, T], BF16)
    ktm = sbuf(ctx, "p3_ktm", [128, 2, 2, NCH, D0], BF16)
    vtm = sbuf(ctx, "p3_vtm", [128, 2, 2, NCH, D0], BF16)
    sF = sbuf(ctx, "p3_sF", [64, 2, D0], F32)
    sB = sbuf(ctx, "p3_sB", [64, 2, D0], BF16)
    Pj = sbuf(ctx, "p3_P", [128, 2, 7, 128], BF16)
    Qj = sbuf(ctx, "p3_Q", [128, 2, 7, 128], BF16)
    zF = sbuf(ctx, "p3_zF", [128, 2, D0], F32)
    zB = sbuf(ctx, "p3_zB", [128, 2, D0], BF16)
    gm = sbuf(ctx, "p3_gm", [128, 2, 128], BF16)
    ytmp = sbuf(ctx, "p3_yt", [128, 2, D0], F32)
    ybuf = sbuf(ctx, "p3_yb", [128, NCH, DM], BF16)

    NPAIR = BSEQ * H0 // 2
    TPC, VPC = 25, 36

    def pair_tv(p):
        return TPC * NCH * (p + 1), (VPC * NCH + 2) * (p + 1)

    def scalar_body(eng):
        eng.wait_ge(ctx.sem("pb"), ctx.pb_total)
        for p in range(NPAIR):
            seq, hp = p // (H0 // 2), p % (H0 // 2)
            par = p % 2
            if p >= 2:
                tprev, vprev = pair_tv(p - 2)
                eng.wait_ge(ctx.sem("p3_tn0"), ctx.cnt["p3_tn0_base"] + TPC * NCH * (p - 1))
                eng.wait_ge(ctx.sem("p3_tn1"), ctx.cnt["p3_tn1_base"] + TPC * NCH * (p - 1))
            eng.dma_start_transpose(
                kfm[:, par, :], dram["kh0f"][seq][:, 128 * hp:128 * (hp + 1)]) \
                .then_inc(ctx.sem("p3_ldt"), 16)
            eng.dma_start_transpose(
                qfm[:, par, :], dram["qh0f"][seq][:, 128 * hp:128 * (hp + 1)]) \
                .then_inc(ctx.sem("p3_ldt"), 16)
            eng.wait_ge(ctx.sem("p3_ldt"), ctx.cnt["p3_ldt"] + 32)
            eng.dma_start(out=kfmB[:, par, :], in_=kfm[64:128, par, :]) \
                .then_inc(ctx.sem("p3_ldt"), 16)
            eng.dma_start(out=qfmB[:, par, :], in_=qfm[64:128, par, :]) \
                .then_inc(ctx.sem("p3_ldt"), 16)
            ctx.cnt["p3_ldt"] += 64

    def sync_body(sync):
        sync.wait_ge(ctx.sem("pb"), ctx.pb_total)
        for p in range(NPAIR):
            seq, hp = p // (H0 // 2), p % (H0 // 2)
            par = p % 2
            if p >= 2:
                sync.wait_ge(ctx.sem("p3_tn0"), ctx.cnt["p3_tn0_base"] + TPC * NCH * (p - 1))
                sync.wait_ge(ctx.sem("p3_tn1"), ctx.cnt["p3_tn1_base"] + TPC * NCH * (p - 1))
            for s in range(2):
                h = 2 * hp + s
                sync.dma_start(
                    out=ktm[:, s, par],
                    in_=dram["kh0"][seq, h].rearrange("(c p) d -> p c d", p=128)) \
                    .then_inc(ctx.sem("p3_ld"), 16)
                sync.dma_start(
                    out=vtm[:, s, par],
                    in_=dram["vh0"][seq, h].rearrange("(c p) d -> p c d", p=128)) \
                    .then_inc(ctx.sem("p3_ld"), 16)
            ctx.cnt["p3_ld"] += 64

    def tensor_chunk(eng, p, s, c, tb, vb):
        par = p % 2
        tn = ctx.sem(f"p3_tn{s}")
        ve = ctx.sem(f"p3_ve{s}")
        off = 2 if c == 0 else 0
        A, B, C, E = bank[4 * s], bank[4 * s + 1], bank[4 * s + 2], bank[4 * s + 3]
        if s == 0:
            kf = kfm[0:64, par, c * CS:(c + 1) * CS]
            qf = qfm[0:64, par, c * CS:(c + 1) * CS]
        else:
            kf = kfmB[:, par, c * CS:(c + 1) * CS]
            qf = qfmB[:, par, c * CS:(c + 1) * CS]
        kt = ktm[:, s, par, c, :]
        sb_ = sB[:, s, :]
        eng.wait_ge(ve, vb + off)
        eng.matmul(A, kf, kf, start=True, stop=True).then_inc(tn, 1)
        eng.matmul(C[:, 0:D0], kf, sb_, start=True, stop=True).then_inc(tn, 1)
        eng.matmul(B, kf, qf, start=True, stop=True).then_inc(tn, 1)
        for j in range(6):
            eng.wait_ge(ve, vb + off + 6 + 2 * j)
            eng.matmul(A, Qj[:, s, j, :], Pj[:, s, j, :], start=True, stop=True).then_inc(tn, 1)
            eng.matmul(B, Pj[:, s, j, :], Qj[:, s, j, :], start=True, stop=True).then_inc(tn, 1)
        for i, j in enumerate(range(6, -1, -1)):
            eng.wait_ge(ve, vb + off + 18 + 2 * i)
            eng.matmul(C[:, 0:D0], Pj[:, s, j, :], zB[:, s, :], start=True, stop=True).then_inc(tn, 1)
        eng.matmul(E[:, 0:D0], qf, sb_, start=True, stop=True).then_inc(tn, 1)
        eng.wait_ge(ve, vb + off + 32)
        eng.matmul(C[:, 0:D0], gm[:, s, :], zB[:, s, :], start=True, stop=True).then_inc(tn, 1)
        eng.wait_ge(ve, vb + off + 33)
        eng.matmul(E[0:64, 0:D0], kt, zB[:, s, :], start=True, stop=True).then_inc(tn, 1)

    def tensor_body(eng):
        eng.wait_ge(ctx.sem("pb"), ctx.pb_total)
        tb = [ctx.cnt["p3_tn0_base"], ctx.cnt["p3_tn1_base"]]
        vb = [ctx.cnt["p3_ve0_base"], ctx.cnt["p3_ve1_base"]]
        for p in range(NPAIR):
            eng.wait_ge(ctx.sem("p3_ld"), ctx.cnt["p3_ld_base"] + 64 * (p + 1))
            eng.wait_ge(ctx.sem("p3_ldt"), ctx.cnt["p3_ldt_base"] + 64 * (p + 1))
            for c in range(NCH):
                for s in range(2):
                    tensor_chunk(eng, p, s, c, tb[s], vb[s])
                    tb[s] += TPC
                    vb[s] += VPC + (2 if c == 0 else 0)

    def vector_chunk(eng, p, s, c, tb, vb):
        par = p % 2
        tn = ctx.sem(f"p3_tn{s}")
        ve = ctx.sem(f"p3_ve{s}")
        h = 2 * (p % (H0 // 2)) + s
        A, B, C, E = bank[4 * s], bank[4 * s + 1], bank[4 * s + 2], bank[4 * s + 3]

        def I(instr):
            instr.then_inc(ve, 1)
        if c == 0:
            eng.wait_ge(tn, tb)
            I(eng.memset(sF[:, s, :], 0.0))
            I(eng.memset(sB[:, s, :], 0.0))
        eng.wait_ge(tn, tb + 1)
        I(eng.tensor_mul(Pj[:, s, 0, :], A, nSU[:, h * 128:(h + 1) * 128]))
        I(eng.tensor_mul(Qj[:, s, 0, :], A, nSL[:, h * 128:(h + 1) * 128]))
        eng.wait_ge(tn, tb + 2)
        I(eng.tensor_scalar(zF[:, s, :], C[:, 0:D0], abcols[:, h:h + 1], None, ALU.mult))
        I(eng.scalar_tensor_tensor(zF[:, s, :], vtm[:, s, par, c, :],
                                   abcols[:, 16 + h:17 + h], zF[:, s, :],
                                   ALU.mult, ALU.add))
        I(eng.tensor_copy(zB[:, s, :], zF[:, s, :]))
        eng.wait_ge(tn, tb + 3)
        I(eng.tensor_mul(gm[:, s, :], B, SUm))
        for j in range(6):
            eng.wait_ge(tn, tb + 3 + 2 * j + 1)
            I(eng.tensor_copy(Pj[:, s, j + 1, :], A))
            eng.wait_ge(tn, tb + 3 + 2 * j + 2)
            I(eng.tensor_copy(Qj[:, s, j + 1, :], B))
        for i in range(7):
            eng.wait_ge(tn, tb + 15 + (i + 1))
            I(eng.tensor_add(zF[:, s, :], zF[:, s, :], C[:, 0:D0]))
            I(eng.tensor_copy(zB[:, s, :], zF[:, s, :]))
        eng.wait_ge(tn, tb + 23)
        I(eng.tensor_copy(ytmp[:, s, :], E[:, 0:D0]))
        eng.wait_ge(tn, tb + 24)
        I(eng.tensor_add(ybuf[:, c, h * D0:(h + 1) * D0], ytmp[:, s, :], C[:, 0:D0]))
        eng.wait_ge(tn, tb + 25)
        I(eng.tensor_add(sF[:, s, :], sF[:, s, :], E[0:64, 0:D0]))
        I(eng.tensor_copy(sB[:, s, :], sF[:, s, :]))

    def vector_body(eng):
        eng.wait_ge(ctx.sem("pb"), ctx.pb_total)
        tb = [ctx.cnt["p3_tn0_base"], ctx.cnt["p3_tn1_base"]]
        vb = [ctx.cnt["p3_ve0_base"], ctx.cnt["p3_ve1_base"]]
        for p in range(NPAIR):
            if p % (H0 // 2) == 0 and p > 0:
                eng.wait_ge(ctx.sem("p3_yst"), ctx.cnt["p3_yst_base"] + 16 * (p // (H0 // 2)))
            for c in range(NCH):
                for s in range(2):
                    vector_chunk(eng, p, s, c, tb[s], vb[s])
                    tb[s] += TPC
                    vb[s] += VPC + (2 if c == 0 else 0)

    def gp_body(eng):
        eng.wait_ge(ctx.sem("pb"), ctx.pb_total)
        for seq in range(BSEQ):
            pend = (seq + 1) * (H0 // 2)
            eng.wait_ge(ctx.sem("p3_ve0"), ctx.cnt["p3_ve0_base"] + (VPC * NCH + 2) * pend)
            eng.wait_ge(ctx.sem("p3_ve1"), ctx.cnt["p3_ve1_base"] + (VPC * NCH + 2) * pend)
            eng.dma_start(out=dram["ytm0"].rearrange("(q c p) m -> q p c m", q=BSEQ, p=128)[seq],
                          in_=ybuf).then_inc(ctx.sem("p3_yst"), 16)
        eng.wait_ge(ctx.sem("p3_yst"), ctx.cnt["p3_yst_base"] + 16 * BSEQ)
        eng.sem_inc(ctx.sem("pb"), 1)

    for nm in ["p3_tn0", "p3_tn1", "p3_ve0", "p3_ve1", "p3_ld", "p3_ldt", "p3_yst"]:
        ctx.sem(nm)
        ctx.cnt[nm + "_base"] = ctx.cnt[nm]
    block.scalar(scalar_body)
    block.sync(sync_body)
    block.tensor(tensor_body)
    block.vector(vector_body)
    block.gpsimd(gp_body)
    ctx.cnt["p3_tn0"] += TPC * NCH * NPAIR
    ctx.cnt["p3_tn1"] += TPC * NCH * NPAIR
    ctx.cnt["p3_ve0"] += (VPC * NCH + 2) * NPAIR
    ctx.cnt["p3_ve1"] += (VPC * NCH + 2) * NPAIR
    ctx.cnt["p3_yst"] += 16 * BSEQ
    ctx.pb_total += 1


PHASE_EMITTERS["p3"] = emit_scan


def emit_proj_tm(ctx, tag, src_name, w_name, dst_name, bias_name=None):
    nc, dram = ctx.nc, ctx.dram
    w = sbuf(ctx, f"{tag}_w", [128, KC, DM], BF16)
    xf = sbuf(ctx, f"{tag}_xf", [128, KC, T], BF16)
    ob = sbuf(ctx, f"{tag}_ob", [128, 2 * DM], F32)
    brow = sbuf(ctx, f"{tag}_br", [128, DM], F32) if bias_name else None
    onesrow = sbuf(ctx, f"{tag}_or", [1, 128], F32) if bias_name else None
    bsrc = sbuf(ctx, f"{tag}_bs", [1, DM], F32) if bias_name else None
    ps = [psum_t(ctx, f"{tag}_ps{i}", [128, 512]) for i in range(4)]

    run_pipe(ctx, 1, [("sync", lambda e, i: e.dma_start(
        out=w, in_=dram[w_name].rearrange("(kc p) o -> p kc o", p=128)), "dma", 1)])
    if bias_name:
        run_pipe(ctx, 1, [("sync", lambda e, i: [
            e.dma_start(out=bsrc, in_=dram[bias_name]),
            e.dma_start(out=onesrow, in_=dram["c_onesrow"])], "dma", 2)])

        def bc_mm(eng):
            return eng.matmul(ps[0][:, 0:512], onesrow, bsrc[0:1, 0:512], start=True, stop=True)
        solo(ctx, "tensor", bc_mm)
        solo(ctx, "vector", lambda e: e.tensor_copy(brow[:, 0:512], ps[0][:, 0:512]))

        def bc_mm2(eng):
            return eng.matmul(ps[0][:, 0:512], onesrow, bsrc[0:1, 512:1024], start=True, stop=True)
        solo(ctx, "tensor", bc_mm2)
        solo(ctx, "vector", lambda e: e.tensor_copy(brow[:, 512:1024], ps[0][:, 0:512]))

    for seq in range(BSEQ):
        def tr_load(eng, i, seq=seq):
            return eng.dma_start_transpose(
                xf[:, i, :], dram[src_name][seq * T:(seq + 1) * T, i * 128:(i + 1) * 128])
        run_pipe(ctx, KC, [("scalar", tr_load, "dma", 1)], bufs=KC)

        def mm_fn(eng, t, seq=seq):
            r = None
            for oh in range(2):
                for kc in range(KC):
                    r = eng.matmul(ps[(t % 2) * 2 + oh], xf[:, kc, t * 128:(t + 1) * 128],
                                   w[:, kc, oh * 512:(oh + 1) * 512],
                                   start=(kc == 0), stop=(kc == KC - 1))
            return r

        def ev_fn(eng, t, seq=seq):
            sl = ob[:, (t % 2) * DM:(t % 2) * DM + DM]
            if bias_name:
                eng.tensor_add(sl[:, 0:512], ps[(t % 2) * 2], brow[:, 0:512])
                return eng.tensor_add(sl[:, 512:1024], ps[(t % 2) * 2 + 1], brow[:, 512:1024])
            eng.tensor_copy(sl[:, 0:512], ps[(t % 2) * 2])
            return eng.tensor_copy(sl[:, 512:1024], ps[(t % 2) * 2 + 1])

        def st_fn(eng, t, seq=seq):
            sl = ob[:, (t % 2) * DM:(t % 2) * DM + DM]
            return eng.dma_start(
                out=dram[dst_name][seq * T + t * 128:seq * T + (t + 1) * 128, :], in_=sl)
        run_pipe(ctx, 16, [("tensor", mm_fn, "c", 1),
                           ("vector", ev_fn, "c", 1),
                           ("gpsimd", st_fn, "dma", 1)])


def emit_ffn0(ctx):
    nc, dram, block = ctx.nc, ctx.dram, ctx.block
    w1 = sbuf(ctx, "p4_w1", [128, KC, DH], BF16)
    xqf = sbuf(ctx, "p4_xqf", [128, KC, T], BF16)
    isxb = sbuf(ctx, "p4_isxb", [128, T], F32)
    isxr = sbuf(ctx, "p4_isxr", [1, T], F32)
    misc = sbuf(ctx, "p4_misc", [1, 64], F32)
    iswc = sbuf(ctx, "p4_iswc", [128, 2], F32)
    onesrow = sbuf(ctx, "p4_or", [1, 128], F32)
    convw = sbuf(ctx, "p4_cw", [128, CC * 3], F32)
    convb = sbuf(ctx, "p4_cb", [128, CC], F32)
    ucc = sbuf(ctx, "p4_ucc", [128, 2 * (T + 2)], F32)
    hsil = sbuf(ctx, "p4_hs", [128, 2 * T], F32)
    hb = sbuf(ctx, "p4_hb", [128, 2 * T], BF16)
    habs = sbuf(ctx, "p4_habs", [128, 2 * T], F32)
    macc = sbuf(ctx, "p4_macc", [128, T], F32)
    ident = sbuf(ctx, "p4_id", [128, 128], BF16)
    identf = sbuf(ctx, "p4_idf", [128, 128], F32)
    mxc_t = sbuf(ctx, "p4_mxc", [128, 64], F32)
    ps = [psum_t(ctx, f"p4_ps{i}", [128, 512]) for i in range(8)]

    def ld0(eng, i):
        return [eng.dma_start(out=w1, in_=dram["w1qb"].rearrange("(kc p) c -> p kc c", p=128)),

                eng.dma_start(out=misc, in_=dram["miscf"]),
                eng.dma_start(out=onesrow, in_=dram["c_onesrow"]),
                eng.dma_start(out=convw, in_=dram["convw"]),
                eng.dma_start(out=convb, in_=dram["convb"]),
                eng.dma_start(out=ident, in_=dram["c_ident"])]
    run_pipe(ctx, 1, [("sync", ld0, "dma", 6)])

    def bc_isw(eng):
        return eng.matmul(ps[0][:, 0:2], onesrow, misc[0:1, 32:34], start=True, stop=True)
    solo(ctx, "tensor", bc_isw)
    solo(ctx, "vector", lambda e: e.tensor_copy(iswc, ps[0][:, 0:2]))
    solo(ctx, "scalar", lambda e: e.copy(identf, ident))

    for seq in range(BSEQ):
        def tr_load(eng, i, seq=seq):
            return eng.dma_start_transpose(
                xqf[:, i, :], dram["xq0b"][seq * T:(seq + 1) * T, i * 128:(i + 1) * 128])
        run_pipe(ctx, KC, [("scalar", tr_load, "dma", 1)], bufs=KC)

        def isx_bc(eng):
            r = None
            for sp in range(4):
                r = eng.matmul(ps[sp % 2][:, 0:512], onesrow,
                               isxr[0:1, seq * T + sp * 512:seq * T + (sp + 1) * 512],
                               start=True, stop=True)
            return r
        def isx_ld(eng, i, seq=seq):
            return eng.dma_start(out=isxr,
                                 in_=dram["isxT"].rearrange("a b -> (a b)")[None, seq * T:(seq + 1) * T])
        run_pipe(ctx, 1, [("sync", isx_ld, "dma", 1)])

        def isx_bc_mm(eng, sp, seq=seq):
            return eng.matmul(ps[sp % 2][:, 0:512], onesrow,
                              isxr[0:1, sp * 512:(sp + 1) * 512],
                              start=True, stop=True)
        run_pipe(ctx, 4, [("tensor", lambda e, i, seq=seq: isx_bc_mm(e, i, seq), "c", 1),
                          ("vector", lambda e, i: e.tensor_copy(
                              isxb[:, i * 512:(i + 1) * 512], ps[i % 2][:, 0:512]), "c", 1)])

        def u_mm(eng, i, seq=seq):
            cc = i
            r = None
            for sp in range(4):
                for kc in range(KC):
                    r = eng.matmul(ps[(i % 2) * 2 + sp % 2][:, 0:512] if False else ps[2 * (i % 2) + (sp % 2)],
                                   xqf[:, kc, sp * 512:(sp + 1) * 512].rearrange("p c -> p c") if False else w1[:, kc, cc * 128:(cc + 1) * 128],
                                   xqf[:, kc, sp * 512:(sp + 1) * 512],
                                   start=(kc == 0), stop=(kc == KC - 1))
            return r
        def u_mm2(eng, i, seq=seq):
            cc = i
            r = None
            for sp in range(4):
                for kc in range(KC):
                    r = eng.matmul(ps[(i % 2) * 4 + sp], w1[:, kc, cc * 128:(cc + 1) * 128],
                                   xqf[:, kc, sp * 512:(sp + 1) * 512],
                                   start=(kc == 0), stop=(kc == KC - 1))
            return r

        def u_ev(eng, i, seq=seq):
            cc = i
            sl = ucc[:, (i % 2) * (T + 2):(i % 2) * (T + 2) + T + 2]
            if i < 2:
                eng.memset(sl[:, 0:1], 0.0)
                eng.memset(sl[:, T + 1:T + 2], 0.0)
            for sp in range(4):
                eng.scalar_tensor_tensor(
                    sl[:, 1 + sp * 512:1 + (sp + 1) * 512], ps[(i % 2) * 4 + sp], iswc[:, 0:1],
                    isxb[:, sp * 512:(sp + 1) * 512], ALU.mult, ALU.mult)
            hs = hsil[:, (i % 2) * T:(i % 2) * T + T]
            w0 = convw[:, 3 * cc:3 * cc + 1]
            w1c = convw[:, 3 * cc + 1:3 * cc + 2]
            w2c = convw[:, 3 * cc + 2:3 * cc + 3]
            bc = convb[:, cc:cc + 1]
            eng.tensor_scalar(hs, sl[:, 0:T], w0, bc, ALU.mult, ALU.add)
            eng.scalar_tensor_tensor(hs, sl[:, 1:T + 1], w1c, hs, ALU.mult, ALU.add)
            return eng.scalar_tensor_tensor(hs, sl[:, 2:T + 2], w2c, hs, ALU.mult, ALU.add)

        def silu_cast(eng, i, seq=seq):
            hs = hsil[:, (i % 2) * T:(i % 2) * T + T]
            eng.activation(hs, hs, AF.Silu)
            return eng.copy(hb[:, (i % 2) * T:(i % 2) * T + T], hs)

        def hmax(eng, i, seq=seq):
            hs = hsil[:, (i % 2) * T:(i % 2) * T + T]
            if i == 0:
                return eng.tensor_tensor(macc[:, seq, :], hs, hs, ALU.abs_max)
            return eng.tensor_tensor(macc[:, seq, :], macc[:, seq, :], hs, ALU.abs_max)

        def h_st(eng, i, seq=seq):
            return eng.dma_start(out=dram["hbf"][i, :, seq * T:(seq + 1) * T],
                                 in_=hb[:, (i % 2) * T:(i % 2) * T + T])
        run_pipe(ctx, CC, [("tensor", u_mm2, "c", 1),
                           ("vector", u_ev, "c", 1),
                           ("scalar", silu_cast, "c", 1),
                           ("sync", h_st, "dma", 1)], bufs=2)
        def hmax_pipe(eng, i, seq=seq):
            return hmax(eng, i, seq)

    mxc = None
    for seq in range(BSEQ):
        def mx_ld(eng, i, seq=seq):
            return eng.dma_start(out=hb[:, (i % 2) * T:(i % 2) * T + T],
                                 in_=dram["hbf"][i, :, seq * T:(seq + 1) * T])

        def mx_abs(eng, i, seq=seq):
            hs = hb[:, (i % 2) * T:(i % 2) * T + T]
            return eng.activation(habs[:, (i % 2) * T:(i % 2) * T + T], hs, AF.Abs)

        def mx_acc(eng, i, seq=seq):
            ha = habs[:, (i % 2) * T:(i % 2) * T + T]
            if i == 0:
                return eng.tensor_copy(macc, ha)
            return eng.tensor_max(macc, macc, ha)
        run_pipe(ctx, CC, [("sync", mx_ld, "dma", 1),
                           ("scalar", mx_abs, "c", 1),
                           ("vector", mx_acc, "c", 1)])

        def mtr(eng, i, seq=seq):
            return eng.transpose(ps[i % 4][:, 0:128], macc[:, i * 128:(i + 1) * 128], identf)

        def mred(eng, i, seq=seq):
            return eng.tensor_reduce(mxc_t[:, seq * 16 + i:seq * 16 + i + 1],
                                     ps[i % 4][:, 0:128], AX.X, ALU.max)
        run_pipe(ctx, 16, [("tensor", mtr, "c", 1), ("vector", mred, "c", 1)], bufs=4)

    ishc = sbuf(ctx, "p4_ishc", [128, 64], F32)

    def mk_ishc(eng):
        eng.tensor_scalar(ishc, mxc_t, 1e-5, None, ALU.max)
        return eng.tensor_scalar(ishc, ishc, 1.0 / 127.0, None, ALU.mult)
    solo(ctx, "vector", mk_ishc)

    def mxc_tr(eng):
        return eng.transpose(ps[0][0:64, 0:128], mxc_t, identf)
    solo(ctx, "tensor", mxc_tr)
    solo(ctx, "vector", lambda e: e.tensor_copy(macc[0:64, 0:128], ps[0][0:64, 0:128]))

    def mx_st(eng):
        d = eng.dma_start(out=dram["mxrow"].rearrange("o (a b) -> (o a) b", b=128),
                          in_=macc[0:64, 0:128])
        d.then_inc(ctx.sem("dma_gpsimd"), 16)
        ctx.cnt["dma_gpsimd"] += 16
        eng.wait_ge(ctx.sem("dma_gpsimd"), ctx.cnt["dma_gpsimd"])
        d2 = eng.dma_start(out=dram["ishc_h"], in_=ishc)
        d2.then_inc(ctx.sem("dma_gpsimd"), 16)
        ctx.cnt["dma_gpsimd"] += 16
        eng.wait_ge(ctx.sem("dma_gpsimd"), ctx.cnt["dma_gpsimd"])
        return None
    solo(ctx, "gpsimd", mx_st)


def emit_ffn0c(ctx):
    nc, dram = ctx.nc, ctx.dram
    w2 = sbuf(ctx, "p4_w2", [128, CC, DM], BF16)
    misc = sbuf(ctx, "c4_misc", [1, 64], F32)
    iswc = sbuf(ctx, "c4_iswc", [128, 2], F32)
    onesrow = sbuf(ctx, "c4_or", [1, 128], F32)
    ishc = sbuf(ctx, "c4_ishc", [128, 64], F32)
    hb = sbuf(ctx, "c4_hb", [128, 2 * T], BF16)
    ps = [psum_t(ctx, f"c4_ps{i}", [128, 512]) for i in range(4)]
    s2b = sbuf(ctx, "p4_s2b", [128, T], F32)
    ishb = sbuf(ctx, "p4_ishb", [128, T], F32)
    s2r = sbuf(ctx, "p4_s2r", [1, T], F32)
    ishr = sbuf(ctx, "p4_ishr", [1, T], F32)
    xq2 = sbuf(ctx, "p4_xq2", [128, CC, 512], BF16)
    hq = sbuf(ctx, "p4_hq", [128, T], F32)
    hqi = sbuf(ctx, "p4_hqi", [128, T], I32)
    xio = sbuf(ctx, "p4_xio", [128, 6 * DM], F32)

    run_pipe(ctx, 1, [("sync", lambda e, i: [
        e.dma_start(out=w2, in_=dram["w2qb"].rearrange("(cc p) o -> p cc o", p=128)),
        e.dma_start(out=misc, in_=dram["miscf"]),
        e.dma_start(out=onesrow, in_=dram["c_onesrow"]),
        e.dma_start(out=ishc, in_=dram["ishc_h"])], "dma", 4)])

    def bc_isw(eng):
        return eng.matmul(ps[0][:, 0:2], onesrow, misc[0:1, 32:34], start=True, stop=True)
    solo(ctx, "tensor", bc_isw)
    solo(ctx, "vector", lambda e: e.tensor_copy(iswc, ps[0][:, 0:2]))

    for seq in range(BSEQ):
        def s2_mk(eng, seqq=seq):
            eng.tensor_scalar(s2r, dram_mx_slice, 1e-5, None, ALU.max) if False else None
            return None
        def s2_ld(eng, i, seq=seq):
            return eng.dma_start(out=s2r, in_=dram["mxrow"][:, seq * T:(seq + 1) * T])
        run_pipe(ctx, 1, [("sync", s2_ld, "dma", 1)])

        def s2_mkv(eng, seq=seq):
            eng.tensor_scalar(s2r, s2r, 1e-5, None, ALU.max)
            eng.tensor_scalar(ishr, s2r, 1.0 / 127.0, None, ALU.mult)
            eng.reciprocal(s2r, s2r)
            return eng.tensor_scalar(s2r, s2r, 127.0, None, ALU.mult)
        solo(ctx, "vector", s2_mkv)

        def s2_bc(eng, i, seq=seq):
            a = eng.matmul(ps[i % 2][:, 0:512], onesrow, s2r[0:1, i * 512:(i + 1) * 512],
                           start=True, stop=True)
            b = eng.matmul(ps[2 + i % 2][:, 0:512], onesrow, ishr[0:1, i * 512:(i + 1) * 512],
                           start=True, stop=True)
            return b

        def s2_ev(eng, i, seq=seq):
            eng.tensor_copy(s2b[:, i * 512:(i + 1) * 512], ps[i % 2][:, 0:512])
            return eng.tensor_copy(ishb[:, i * 512:(i + 1) * 512], ps[2 + i % 2][:, 0:512])
        run_pipe(ctx, 4, [("tensor", s2_bc, "c", 1), ("vector", s2_ev, "c", 1)])

        for sp in range(4):
            tsl = slice(seq * T + sp * 512, seq * T + (sp + 1) * 512)
            ssl = slice(sp * 512, (sp + 1) * 512)

            def q_ld(eng, i, tsl=tsl):
                return eng.dma_start(out=hb[:, (i % 2) * T:(i % 2) * T + 512],
                                     in_=dram["hbf"][i, :, tsl])

            def q_mul(eng, i, ssl=ssl):
                sl = hb[:, (i % 2) * T:(i % 2) * T + 512]
                q = hq[:, (i % 2) * 512:(i % 2) * 512 + 512]
                qi = hqi[:, (i % 2) * 512:(i % 2) * 512 + 512]
                eng.tensor_tensor(q, sl, s2b[:, ssl], ALU.mult)
                eng.tensor_scalar(q, q, 127.0, None, ALU.min)
                return eng.tensor_scalar(qi, q, -128.0, None, ALU.max)

            def q_cast(eng, i):
                return eng.copy(xq2[:, i, :], hqi[:, (i % 2) * 512:(i % 2) * 512 + 512])
            run_pipe(ctx, CC, [("sync", q_ld, "dma", 1),
                               ("vector", q_mul, "c", 1),
                               ("scalar", q_cast, "c", 1)])

            def f2_ld(eng, i, seq=seq, sp=sp):
                r0 = seq * T + sp * 512 + i * 128
                a = eng.dma_start(out=xio[:, (i % 2) * 3 * DM:(i % 2) * 3 * DM + DM],
                                  in_=dram["x"][r0:r0 + 128, :])
                b = eng.dma_start(out=xio[:, (i % 2) * 3 * DM + DM:(i % 2) * 3 * DM + 2 * DM],
                                  in_=dram["attn0"][r0:r0 + 128, :])
                return [a, b]

            def f2_mm(eng, i, seq=seq, sp=sp):
                r = None
                for oh in range(2):
                    for cc in range(CC):
                        r = eng.matmul(ps[(i % 2) * 2 + oh], xq2[:, cc, i * 128:(i + 1) * 128],
                                       w2[:, cc, oh * 512:(oh + 1) * 512],
                                       start=(cc == 0), stop=(cc == CC - 1))
                return r

            def f2_ev(eng, i, seq=seq, sp=sp):
                b = (i % 2) * 3 * DM
                tcol = ishc[:, seq * 16 + sp * 4 + i:seq * 16 + sp * 4 + i + 1]
                o = xio[:, b + 2 * DM:b + 3 * DM]
                for oh in range(2):
                    osl = o[:, oh * 512:(oh + 1) * 512]
                    eng.tensor_scalar(osl, ps[(i % 2) * 2 + oh], tcol, iswc[:, 1:2],
                                      ALU.mult, ALU.mult)
                    eng.tensor_add(osl, osl, xio[:, b + oh * 512:b + oh * 512 + 512])
                    r = eng.tensor_add(osl, osl, xio[:, b + DM + oh * 512:b + DM + oh * 512 + 512])
                return r

            def f2_st(eng, i, seq=seq, sp=sp):
                b = (i % 2) * 3 * DM
                r0 = seq * T + sp * 512 + i * 128
                return eng.dma_start(out=dram["x1"][r0:r0 + 128, :],
                                     in_=xio[:, b + 2 * DM:b + 3 * DM])
            run_pipe(ctx, 4, [("sync", f2_ld, "dma", 2),
                              ("tensor", f2_mm, "c", 1),
                              ("vector", f2_ev, "c", 1),
                              ("gpsimd", f2_st, "dma", 1)])


PHASE_EMITTERS.update({
    "p3b": lambda ctx: emit_proj_tm(ctx, "p3b", "ytm0", "wo0b", "attn0"),
    "p4": emit_ffn0,
    "p4c": emit_ffn0c,
    "p7a": lambda ctx: emit_proj_tm(ctx, "p7a", "oln1b", "wo1b", "attn1", bias_name="bo1"),
})


def emit_qkv1(ctx):
    nc, dram = ctx.nc, ctx.dram
    wq = sbuf(ctx, "p5b_wq", [128, KC, DM], BF16)
    wk = sbuf(ctx, "p5b_wk", [128, KC, DM], BF16)
    wv = sbuf(ctx, "p5b_wv", [128, KC, DM], BF16)
    xnf = sbuf(ctx, "p5b_xnf", [128, KC, T], BF16)
    ob = sbuf(ctx, "p5b_ob", [128, 2 * DM], BF16)
    ps = [psum_t(ctx, f"p5b_ps{i}", [128, 512]) for i in range(4)]
    wl = [("wq1b", wq), ("wk1b", wk), ("wv1b", wv)]
    run_pipe(ctx, 3, [("sync", lambda e, i: e.dma_start(
        out=wl[i][1], in_=dram[wl[i][0]].rearrange("(kc p) o -> p kc o", p=128)),
        "dma", 1)], bufs=3)

    for seq in range(BSEQ):
        def tr_load(eng, i, seq=seq):
            return eng.dma_start_transpose(
                xnf[:, i, :], dram["xn1b"][seq * T:(seq + 1) * T, i * 128:(i + 1) * 128])
        run_pipe(ctx, KC, [("scalar", tr_load, "dma", 1)], bufs=KC)

        items = [(p, oc, sp) for p in range(2) for oc in range(8) for sp in range(4)]

        def fm_mm(eng, i, seq=seq):
            p, oc, sp = items[i]
            w = [wq, wk][p]
            r = None
            for kc in range(KC):
                r = eng.matmul(ps[i % 2], w[:, kc, oc * 128:(oc + 1) * 128],
                               xnf[:, kc, sp * 512:(sp + 1) * 512],
                               start=(kc == 0), stop=(kc == KC - 1))
            return r

        def fm_ev(eng, i, seq=seq):
            return eng.tensor_copy(ob[:, (i % 2) * DM:(i % 2) * DM + 512], ps[i % 2])

        def fm_st(eng, i, seq=seq):
            p, oc, sp = items[i]
            dst = dram["q1f"] if p == 0 else dram["k1f"]
            return eng.dma_start(
                out=dst[seq][oc * 128:(oc + 1) * 128, sp * 512:(sp + 1) * 512],
                in_=ob[:, (i % 2) * DM:(i % 2) * DM + 512])
        run_pipe(ctx, len(items), [("tensor", fm_mm, "c", 1),
                                   ("vector", fm_ev, "c", 1),
                                   ("gpsimd", fm_st, "dma", 1)])

        def v_mm(eng, t, seq=seq):
            r = None
            for oh in range(2):
                for kc in range(KC):
                    r = eng.matmul(ps[(t % 2) * 2 + oh], xnf[:, kc, t * 128:(t + 1) * 128],
                                   wv[:, kc, oh * 512:(oh + 1) * 512],
                                   start=(kc == 0), stop=(kc == KC - 1))
            return r

        def v_ev(eng, t, seq=seq):
            sl = ob[:, (t % 2) * DM:(t % 2) * DM + DM]
            eng.tensor_copy(sl[:, 0:512], ps[(t % 2) * 2])
            return eng.tensor_copy(sl[:, 512:1024], ps[(t % 2) * 2 + 1])

        def v_st(eng, t, seq=seq):
            sl = ob[:, (t % 2) * DM:(t % 2) * DM + DM]
            return eng.dma_start(
                out=dram["v1h"][seq][:, t * 128:(t + 1) * 128, :].rearrange("h p d -> p h d"),
                in_=sl.rearrange("p (h d) -> p h d", h=H1))
        run_pipe(ctx, 16, [("tensor", v_mm, "c", 1),
                           ("vector", v_ev, "c", 1),
                           ("gpsimd", v_st, "dma", 1)])


def emit_attn1(ctx):
    nc, dram, block = ctx.nc, ctx.dram, ctx.block
    thc = sbuf(ctx, "p6_th", [128, H1], F32)
    dec = sbuf(ctx, "p6_dec", [128, NCH, 128], F32)
    etile = sbuf(ctx, "p6_et", [128, 128], F32)
    d0 = sbuf(ctx, "p6_d0", [128, 128], F32)
    caus = sbuf(ctx, "p6_ca", [128, 128], F32)
    ndia = sbuf(ctx, "p6_nd", [128, 128], F32)
    ident = sbuf(ctx, "p6_id", [128, 128], BF16)
    onesrow = sbuf(ctx, "p6_or", [1, 128], F32)
    th_s = sbuf(ctx, "p6_ths", [1, H1], F32)
    dslots = sbuf(ctx, "p6_dsl", [128, H1 * 128], F32)
    qs = sbuf(ctx, "p6_qs", [128, T], BF16)
    ks = sbuf(ctx, "p6_ks", [128, T], BF16)
    vs = sbuf(ctx, "p6_vs", [128, NCH, 128], BF16)
    srow = sbuf(ctx, "p6_srow", [128, T], F32)
    prow = sbuf(ctx, "p6_prow", [128, T], BF16)
    ptile = sbuf(ctx, "p6_pt", [128, 16, 128], BF16)
    stat = sbuf(ctx, "p6_stat", [128, 4], F32)
    obuf = sbuf(ctx, "p6_ob", [128, NCH, DM], BF16)
    psA = psum_t(ctx, "p6_psA", [128, 512])
    psB = psum_t(ctx, "p6_psB", [128, 512])
    psT0 = psum_t(ctx, "p6_psT0", [128, 128], BF16)
    psT1 = psum_t(ctx, "p6_psT1", [128, 128], BF16)
    psV = psum_t(ctx, "p6_psV", [128, 128])

    def ld0(eng, i):
        return [eng.dma_start(out=th_s, in_=dram["theta"]),
                eng.dma_start(out=onesrow, in_=dram["c_onesrow"]),
                eng.dma_start(out=d0, in_=dram["c_D0"]),
                eng.dma_start(out=caus, in_=dram["c_causal01"]),
                eng.dma_start(out=ndia, in_=dram["c_negdiag"]),
                eng.dma_start(out=ident, in_=dram["c_ident"])]
    run_pipe(ctx, 1, [("sync", ld0, "dma", 6)])

    def bc_th(eng):
        return eng.matmul(psA[:, 0:H1], onesrow, th_s, start=True, stop=True)
    solo(ctx, "tensor", bc_th)
    solo(ctx, "vector", lambda e: e.tensor_scalar(thc, psA[:, 0:H1], -1.0, None, ALU.mult))

    def d_s1(eng, k):
        eng.tensor_scalar(etile, d0, float(128 * k), None, ALU.add)
        if k == 0:
            return eng.tensor_scalar(etile, etile, 0.0, None, ALU.max)
        return eng.tensor_copy(etile, etile)

    LNC = float(np.log(1.0 / (8 * np.sqrt(128.0))))

    def d_s2(eng, k):
        r = None
        for h in range(H1):
            r = eng.activation(dslots[:, h * 128:(h + 1) * 128], etile, AF.Exp,
                               scale=thc[:, h:h + 1], bias=LNC)
        return r

    def d_s3(eng, k):
        r = eng.tensor_add(dec[:, k, :], dslots[:, 0:128], dslots[:, 128:256])
        for h in range(2, H1):
            r = eng.tensor_add(dec[:, k, :], dec[:, k, :], dslots[:, h * 128:(h + 1) * 128])
        if k == 0:
            r = eng.tensor_mul(dec[:, k, :], dec[:, k, :], caus)
        return r
    run_pipe(ctx, NCH, [("vector", d_s1, "c", 1),
                        ("scalar", d_s2, "c", 1),
                        ("gpsimd", d_s3, "c", 1)], bufs=1)

    NI = BSEQ * H1
    TN, VE, SC, LD, OST = "p6_tn", "p6_ve", "p6_sc", "p6_ld", "p6_ost"
    for nm in [TN, VE, SC, LD, OST]:
        ctx.sem(nm)
        ctx.cnt[nm + "_base"] = ctx.cnt[nm]

    def tile_plan(t):
        nsp = t // 4 + 1
        nst = t + 1
        return nsp, nst, nsp + 2 * nst, 2 * nst + 3

    TPI = sum(tile_plan(t)[2] for t in range(16))
    VPI = sum(tile_plan(t)[3] for t in range(16))

    def decay_done(t, j):
        nsp, nst, _, _ = tile_plan(t)
        cov = min(4 * (j + 1), nst)
        return cov + (1 if cov == nst else 0)

    def sync_body(sync):
        sync.wait_ge(ctx.sem("pb"), ctx.pb_total)
        for i in range(NI):
            seq, h = divmod(i, H1)
            if i >= 1:
                sync.wait_ge(ctx.sem(TN), ctx.cnt[TN + "_base"] + TPI * i)
            sync.dma_start(out=qs, in_=dram["q1f"][seq][h * 128:(h + 1) * 128, :]) \
                .then_inc(ctx.sem(LD), 16)
            sync.dma_start(out=ks, in_=dram["k1f"][seq][h * 128:(h + 1) * 128, :]) \
                .then_inc(ctx.sem(LD), 16)
            sync.dma_start(out=vs, in_=dram["v1h"][seq, h].rearrange("(c p) d -> p c d", p=128)) \
                .then_inc(ctx.sem(LD), 16)
            ctx.cnt[LD] += 48

    def tensor_body(eng):
        eng.wait_ge(ctx.sem("pb"), ctx.pb_total)
        tn, ve, sc = ctx.sem(TN), ctx.sem(VE), ctx.sem(SC)
        tb = ctx.cnt[TN + "_base"]
        vb = ctx.cnt[VE + "_base"]
        scb = ctx.cnt[SC + "_base"]
        for i in range(NI):
            eng.wait_ge(ctx.sem(LD), ctx.cnt[LD + "_base"] + 48 * (i + 1))
            for t in range(16):
                nsp, nst, tinc, vinc = tile_plan(t)
                eng.wait_ge(ve, vb)
                for sp in range(nsp):
                    if sp >= 2:
                        eng.wait_ge(ve, vb + decay_done(t, sp - 2))
                    if P6_MODE == "mm0":
                        eng.nop().then_inc(tn, 1)
                    else:
                        eng.matmul([psA, psB][sp % 2], qs[:, t * 128:(t + 1) * 128],
                                   ks[:, sp * 512:(sp + 1) * 512],
                                   start=True, stop=True).then_inc(tn, 1)
                eng.wait_ge(sc, scb + 1)
                for st in range(nst):
                    if st >= 2:
                        eng.wait_ge(ve, vb + nst + 2 + (st - 1))
                    if P6_MODE in ("notr", "mm0"):
                        eng.nop().then_inc(tn, 1)
                    else:
                        eng.transpose([psT0, psT1][st % 2], prow[:, st * 128:(st + 1) * 128],
                                      ident).then_inc(tn, 1)
                eng.wait_ge(ve, vb + nst + 2 + nst)
                for st in range(nst):
                    if P6_MODE in ("noav", "notr", "mm0"):
                        eng.nop().then_inc(tn, 1)
                    else:
                        eng.matmul(psV, ptile[:, st, :], vs[:, st, :],
                                   start=(st == 0), stop=(st == nst - 1)).then_inc(tn, 1)
                tb += tinc
                vb += vinc
                scb += 1

    def vector_body(eng):
        eng.wait_ge(ctx.sem("pb"), ctx.pb_total)
        tn, ve, sc = ctx.sem(TN), ctx.sem(VE), ctx.sem(SC)
        tb = ctx.cnt[TN + "_base"]
        vb = ctx.cnt[VE + "_base"]
        scb = ctx.cnt[SC + "_base"]
        for i in range(NI):
            seq, h = divmod(i, H1)
            if i % H1 == 0 and i > 0:
                eng.wait_ge(ctx.sem(OST), ctx.cnt[OST + "_base"] + 16 * (i // H1))
            for t in range(16):
                nsp, nst, tinc, vinc = tile_plan(t)
                eng.wait_ge(sc, scb)
                for sp in range(nsp):
                    eng.wait_ge(tn, tb + sp + 1)
                    pbank = [psA, psB][sp % 2]
                    w = min(512, (t + 1) * 128 - sp * 512)
                    for k4 in range((w + 127) // 128):
                        st = sp * 4 + k4
                        kdiag = t - st
                        eng.tensor_mul(srow[:, st * 128:(st + 1) * 128],
                                       pbank[:, k4 * 128:(k4 + 1) * 128],
                                       dec[:, kdiag, :]).then_inc(ve, 1)
                        if kdiag == 0:
                            eng.tensor_add(srow[:, st * 128:(st + 1) * 128],
                                           srow[:, st * 128:(st + 1) * 128],
                                           ndia).then_inc(ve, 1)
                eng.tensor_reduce(stat[:, 0:1], srow[:, 0:(t + 1) * 128], AX.X,
                                  ALU.max, negate=True).then_inc(ve, 1)
                for st in range(nst):
                    eng.wait_ge(tn, tb + nsp + st + 1)
                    eng.tensor_copy(ptile[:, st, :], [psT0, psT1][st % 2]).then_inc(ve, 1)
                eng.wait_ge(tn, tb + nsp + 2 * nst)
                eng.reciprocal(stat[:, 2:3], stat[:, 1:2])
                SPACER["v"](eng)
                eng.tensor_tensor(obuf[:, t, h * 128:(h + 1) * 128], psV,
                                  bcast_col(stat[:, 2:3], 128), ALU.mult).then_inc(ve, 1)
                tb += tinc
                vb += vinc
                scb += 1

    def scalar_body(eng):
        eng.wait_ge(ctx.sem("pb"), ctx.pb_total)
        tn, ve, sc = ctx.sem(TN), ctx.sem(VE), ctx.sem(SC)
        tb = ctx.cnt[TN + "_base"]
        vb = ctx.cnt[VE + "_base"]
        for i in range(NI):
            for t in range(16):
                nsp, nst, tinc, vinc = tile_plan(t)
                if t >= 1:
                    eng.wait_ge(tn, tb)
                eng.wait_ge(ve, vb + nst + 2)
                eng.activation(prow[:, 0:(t + 1) * 128], srow[:, 0:(t + 1) * 128],
                               AF.Exp, bias=stat[:, 0:1],
                               accum_out=stat[:, 1:2]).then_inc(sc, 1)
                tb += tinc
                vb += vinc

    def gp_body(eng):
        eng.wait_ge(ctx.sem("pb"), ctx.pb_total)
        for seq in range(BSEQ):
            iend = (seq + 1) * H1
            eng.wait_ge(ctx.sem(VE), ctx.cnt[VE + "_base"] + VPI * iend)
            eng.dma_start(out=dram["obh"].rearrange("(q c p) m -> q p c m", q=BSEQ, p=128)[seq],
                          in_=obuf).then_inc(ctx.sem(OST), 16)
        eng.wait_ge(ctx.sem(OST), ctx.cnt[OST + "_base"] + 16 * BSEQ)
        eng.sem_inc(ctx.sem("pb"), 1)

    block.sync(sync_body)
    block.tensor(tensor_body)
    block.vector(vector_body)
    block.scalar(scalar_body)
    block.gpsimd(gp_body)
    ctx.cnt[TN] += TPI * NI
    ctx.cnt[VE] += VPI * NI
    ctx.cnt[SC] += 16 * NI
    ctx.cnt[OST] += 16 * BSEQ
    ctx.pb_total += 1

    grow = sbuf(ctx, "p6_gr", [128, DM], F32)
    brow = sbuf(ctx, "p6_br", [128, DM], F32)
    gb_s = sbuf(ctx, "p6_gbs", [1, 2 * DM], F32)
    otile = sbuf(ctx, "p6_ot", [128, 2 * DM], BF16)
    sqb = sbuf(ctx, "p6_sqb", [128, 2 * DM], F32)
    st8 = sbuf(ctx, "p6_st8", [128, NT // 128, 6 * H1], F32)
    oln = sbuf(ctx, "p6_oln", [128, 2 * DM], F32)
    olnb = sbuf(ctx, "p6_olnb", [128, 2 * DM], BF16)

    run_pipe(ctx, 1, [("sync", lambda e, i: [
        e.dma_start(out=gb_s[0:1, 0:DM], in_=dram["gamma"]),
        e.dma_start(out=gb_s[0:1, DM:2 * DM], in_=dram["betaln"])], "dma", 2)])
    for half in range(2):
        def bc_g(eng, half=half):
            eng.matmul(psA, onesrow, gb_s[0:1, half * 512:half * 512 + 512],
                       start=True, stop=True)
            return eng.matmul(psB, onesrow,
                              gb_s[0:1, DM + half * 512:DM + half * 512 + 512],
                              start=True, stop=True)
        solo(ctx, "tensor", bc_g)

        def ev_g(eng, half=half):
            eng.tensor_copy(grow[:, half * 512:half * 512 + 512], psA)
            return eng.tensor_copy(brow[:, half * 512:half * 512 + 512], psB)
        solo(ctx, "vector", ev_g)

    NTT = NT // 128

    def a_ld(eng, i):
        return eng.dma_start(out=otile[:, (i % 2) * DM:(i % 2) * DM + DM],
                             in_=dram["obh"].rearrange("(n p) c -> n p c", p=128)[i])

    def a_sq(eng, i):
        o = otile[:, (i % 2) * DM:(i % 2) * DM + DM]
        return eng.tensor_tensor(sqb[:, (i % 2) * DM:(i % 2) * DM + DM], o, o, ALU.mult)

    def a_stats(eng, i):
        o = otile[:, (i % 2) * DM:(i % 2) * DM + DM].rearrange("p (h d) -> p h d", h=H1)
        q = sqb[:, (i % 2) * DM:(i % 2) * DM + DM].rearrange("p (h d) -> p h d", h=H1)
        s = st8[:, i]
        eng.tensor_reduce(s[:, 0:H1].rearrange("p (h o) -> p h o", o=1), o, AX.X, ALU.add)
        eng.tensor_reduce(s[:, H1:2 * H1].rearrange("p (h o) -> p h o", o=1), q, AX.X, ALU.add)
        eng.tensor_scalar(s[:, 2 * H1:3 * H1], s[:, 0:H1], -1.0 / D1, None, ALU.mult)
        eng.tensor_scalar(s[:, 4 * H1:5 * H1], s[:, H1:2 * H1], 1.0 / D1, None, ALU.mult)
        eng.tensor_mul(s[:, 3 * H1:4 * H1], s[:, 2 * H1:3 * H1], s[:, 2 * H1:3 * H1])
        SPACER["v"](eng)
        return eng.tensor_sub(s[:, 4 * H1:5 * H1], s[:, 4 * H1:5 * H1], s[:, 3 * H1:4 * H1])

    def a_rstd(eng, i):
        s = st8[:, i]
        eng.activation(s[:, 1 * H1:2 * H1], s[:, 4 * H1:5 * H1], AF.Ln, bias=1e-5)
        SPACER["s"](eng)
        return eng.activation(s[:, 5 * H1:6 * H1], s[:, 1 * H1:2 * H1], AF.Exp, scale=-0.5)

    def a_apply(eng, i):
        ot = otile[:, (i % 2) * DM:(i % 2) * DM + DM]
        s = st8[:, i]
        ol = oln[:, (i % 2) * DM:(i % 2) * DM + DM]
        for h in range(H1):
            eng.tensor_scalar(ol[:, h * D1:(h + 1) * D1], ot[:, h * D1:(h + 1) * D1],
                              s[:, 2 * H1 + h:2 * H1 + h + 1],
                              s[:, 5 * H1 + h:5 * H1 + h + 1], ALU.add, ALU.mult)
        eng.tensor_mul(ol, ol, grow)
        eng.tensor_add(ol, ol, brow)
        return eng.tensor_copy(olnb[:, (i % 2) * DM:(i % 2) * DM + DM], ol)

    def a_st(eng, i):
        return eng.dma_start(out=dram["oln1b"].rearrange("(n p) c -> n p c", p=128)[i],
                             in_=olnb[:, (i % 2) * DM:(i % 2) * DM + DM])
    run_pipe(ctx, NTT, [("sync", a_ld, "dma", 1),
                        ("gpsimd", a_sq, "c", 1),
                        ("vector", a_stats, "c", 1),
                        ("scalar", a_rstd, "c", 1)])

    def b_ld(eng, i):
        return eng.dma_start(out=otile[:, (i % 2) * DM:(i % 2) * DM + DM],
                             in_=dram["obh"].rearrange("(n p) c -> n p c", p=128)[i])
    run_pipe(ctx, NTT, [("gpsimd", b_ld, "dma", 1),
                        ("vector", a_apply, "c", 1),
                        ("sync", a_st, "dma", 1)])


PHASE_EMITTERS["p5b"] = emit_qkv1
PHASE_EMITTERS["p6"] = emit_attn1


def emit_ffn1(ctx):
    nc, dram = ctx.nc, ctx.dram
    bw = sbuf(ctx, "p7_bw", [128, KC, DM], BF16)
    rl = sbuf(ctx, "p7_rl", [128, KC, 16], BF16)
    bcat = sbuf(ctx, "p7_bcat", [64, DM], BF16)
    xnf = sbuf(ctx, "p7_xnf", [128, KC, T], BF16)
    bbrow = sbuf(ctx, "p7_bbr", [128, DM], F32)
    rbrow = sbuf(ctx, "p7_rbr", [128, 8], F32)
    bb_s = sbuf(ctx, "p7_bbs", [1, DM], F32)
    rb_s = sbuf(ctx, "p7_rbs", [1, 8], F32)
    onesrow = sbuf(ctx, "p7_or", [1, 128], F32)
    ident = sbuf(ctx, "p7_id", [128, 128], BF16)
    base_f = sbuf(ctx, "p7_basef", [128, 2 * DM], F32)
    rlbuf = sbuf(ctx, "p7_rlb", [128, 16, 16], F32)
    egate = sbuf(ctx, "p7_eg", [128, 16, 8], F32)
    gsum = sbuf(ctx, "p7_gs", [128, 16, 4], F32)
    gateb = sbuf(ctx, "p7_gb", [128, 16, 8], BF16)
    zt = sbuf(ctx, "p7_zt", [128, 16, 64], BF16)
    ztr = sbuf(ctx, "p7_ztr", [64, 16, 128], BF16)
    xio = sbuf(ctx, "p7_xio", [128, 6 * DM], F32)
    psA = psum_t(ctx, "p7_psA", [128, 512])
    psB = psum_t(ctx, "p7_psB", [128, 512])
    psC = psum_t(ctx, "p7_psC", [128, 512])
    psD = psum_t(ctx, "p7_psD", [128, 512])
    psR = psum_t(ctx, "p7_psR", [128, 2, 16])
    psZ0 = psum_t(ctx, "p7_psZ0", [64, 128], BF16)
    psZ1 = psum_t(ctx, "p7_psZ1", [64, 128], BF16)

    def ld0(eng, i):
        return [eng.dma_start(out=bw, in_=dram["basewb"].rearrange("(kc p) o -> p kc o", p=128)),
                eng.dma_start(out=rl, in_=dram["routlorab"].rearrange("(kc p) o -> p kc o", p=128)),
                eng.dma_start(out=bcat, in_=dram["bcatb"]),
                eng.dma_start(out=bb_s, in_=dram["baseb"]),
                eng.dma_start(out=rb_s, in_=dram["routb"]),
                eng.dma_start(out=onesrow, in_=dram["c_onesrow"]),
                eng.dma_start(out=ident, in_=dram["c_ident"])]
    run_pipe(ctx, 1, [("sync", ld0, "dma", 7)])
    for half in range(2):
        def bc_b(eng, half=half):
            return eng.matmul(psA, onesrow, bb_s[0:1, half * 512:half * 512 + 512],
                              start=True, stop=True)
        solo(ctx, "tensor", bc_b)
        solo(ctx, "vector", lambda e, half=half: e.tensor_copy(
            bbrow[:, half * 512:half * 512 + 512], psA))

    def bc_r(eng):
        return eng.matmul(psR[:, 0, 0:8], onesrow, rb_s, start=True, stop=True)
    solo(ctx, "tensor", bc_r)
    solo(ctx, "vector", lambda e: e.tensor_copy(rbrow, psR[:, 0, 0:8]))

    for seq in range(BSEQ):
        def tr_load(eng, i, seq=seq):
            return eng.dma_start_transpose(
                xnf[:, i, :], dram["xn1b"][seq * T:(seq + 1) * T, i * 128:(i + 1) * 128])
        run_pipe(ctx, KC, [("scalar", tr_load, "dma", 1)], bufs=KC)

        def p1_ld(eng, t, seq=seq):
            b = (t % 2) * 3 * DM
            r0 = seq * T + t * 128
            return [eng.dma_start(out=xio[:, b:b + DM], in_=dram["x1"][r0:r0 + 128, :]),
                    eng.dma_start(out=xio[:, b + DM:b + 2 * DM],
                                  in_=dram["attn1"][r0:r0 + 128, :])]

        def p1_mm(eng, t, seq=seq):
            for oh in range(2):
                for kc in range(KC):
                    eng.matmul([[psA, psC], [psB, psD]][oh][t % 2],
                               xnf[:, kc, t * 128:(t + 1) * 128],
                               bw[:, kc, oh * 512:(oh + 1) * 512],
                               start=(kc == 0), stop=(kc == KC - 1))
            r = None
            for kc in range(KC):
                r = eng.matmul(psR[:, t % 2, :], xnf[:, kc, t * 128:(t + 1) * 128],
                               rl[:, kc, :], start=(kc == 0), stop=(kc == KC - 1))
            return r

        def p1_ev(eng, t, seq=seq):
            b = (t % 2) * 3 * DM
            o = xio[:, b + 2 * DM:b + 3 * DM]
            eng.tensor_copy(rlbuf[:, t, :], psR[:, t % 2, :])
            for oh in range(2):
                osl = o[:, oh * 512:(oh + 1) * 512]
                pbank = [[psA, psC], [psB, psD]][oh][t % 2]
                eng.tensor_add(osl, pbank, bbrow[:, oh * 512:(oh + 1) * 512])
                eng.tensor_add(osl, osl, xio[:, b + oh * 512:b + oh * 512 + 512])
                eng.tensor_add(osl, osl, xio[:, b + DM + oh * 512:b + DM + oh * 512 + 512])
            eng.tensor_add(rlbuf[:, t, 0:8], rlbuf[:, t, 0:8], rbrow)
            SPACER["v"](eng)
            return eng.tensor_reduce(gsum[:, t, 0:1], rlbuf[:, t, 0:8], AX.X,
                                     ALU.max, negate=True)

        def p1_exp(eng, t, seq=seq):
            return eng.activation(egate[:, t, :], rlbuf[:, t, 0:8], AF.Exp,
                                  bias=gsum[:, t, 0:1], accum_out=gsum[:, t, 1:2])

        def p1_st(eng, t, seq=seq):
            b = (t % 2) * 3 * DM
            r0 = seq * T + t * 128
            return eng.dma_start(out=dram["out"][r0:r0 + 128, :],
                                 in_=xio[:, b + 2 * DM:b + 3 * DM])
        run_pipe(ctx, 16, [("sync", p1_ld, "dma", 2),
                           ("tensor", p1_mm, "c", 1),
                           ("vector", p1_ev, "c", 1),
                           ("scalar", p1_exp, "c", 1),
                           ("gpsimd", p1_st, "dma", 1)])

        def p2_g(eng, t, seq=seq):
            eng.reciprocal(gsum[:, t, 2:3], gsum[:, t, 1:2])
            SPACER["v"](eng)
            eng.tensor_tensor(gateb[:, t, :], egate[:, t, :],
                              bcast_col(gsum[:, t, 2:3], 8), ALU.mult)
            SPACER["v"](eng)
            r = None
            for e8 in range(8):
                r = eng.tensor_tensor(zt[:, t, e8 * 8:(e8 + 1) * 8], rlbuf[:, t, 8:16],
                                      bcast_col(gateb[:, t, e8:e8 + 1], 8), ALU.mult)
            return r

        def p2_tr(eng, t, seq=seq):
            return eng.transpose([psZ0, psZ1][t % 2], zt[:, t, 0:64], ident)

        def p2_ev(eng, t, seq=seq):
            return eng.copy(ztr[0:64, t, :], [psZ0, psZ1][t % 2])
        run_pipe(ctx, 16, [("vector", p2_g, "c", 1),
                           ("tensor", p2_tr, "c", 1),
                           ("scalar", p2_ev, "c", 1)])

        def p3_ld(eng, t, seq=seq):
            r0 = seq * T + t * 128
            return eng.dma_start(out=xio[:, (t % 2) * 3 * DM:(t % 2) * 3 * DM + DM],
                                 in_=dram["out"][r0:r0 + 128, :])

        def p3_mm(eng, t, seq=seq):
            eng.matmul([psA, psC][t % 2], ztr[0:64, t, :], bcat[:, 0:512],
                       start=True, stop=True)
            return eng.matmul([psB, psD][t % 2], ztr[0:64, t, :], bcat[:, 512:1024],
                              start=True, stop=True)

        def p3_ev(eng, t, seq=seq):
            b = (t % 2) * 3 * DM
            o = xio[:, b + 2 * DM:b + 3 * DM]
            r = None
            for oh in range(2):
                osl = o[:, oh * 512:(oh + 1) * 512]
                pbank = [[psA, psC], [psB, psD]][oh][t % 2]
                eng.tensor_scalar(osl, pbank, 2.0, None, ALU.mult)
                r = eng.tensor_add(osl, osl, xio[:, b + oh * 512:b + oh * 512 + 512])
            return r

        def p3_st(eng, t, seq=seq):
            b = (t % 2) * 3 * DM
            r0 = seq * T + t * 128
            return eng.dma_start(out=dram["out"][r0:r0 + 128, :],
                                 in_=xio[:, b + 2 * DM:b + 3 * DM])
        run_pipe(ctx, 16, [("sync", p3_ld, "dma", 1),
                           ("tensor", p3_mm, "c", 1),
                           ("vector", p3_ev, "c", 1),
                           ("gpsimd", p3_st, "dma", 1)])


PHASE_EMITTERS["p7b"] = emit_ffn1
